# version 04
# baseline (speedup 1.0000x reference)
import numpy as np
import ml_dtypes
import concourse.bass as bass
import concourse.mybir as mybir
from concourse.bass_utils import run_bass_kernel_spmd

F32 = mybir.dt.float32
BF16 = mybir.dt.bfloat16
AF = mybir.ActivationFunctionType
ALU = mybir.AluOpType
AX = mybir.AxisListType

NCORES = 8
B, S, D = 4, 4096, 1024
NH, HDIM = 16, 64
NE, NEX, TOPK = 16, 17, 4
TOK, NSLOT, CHK = 2048, 8, 256
CH_A = [0, 15, 2, 13, 4, 11, 6, 9]
CH_B = [1, 14, 3, 12, 5, 10, 7, 8]
SLOT_KT = [max(2 * (a + 1), 2 * (b + 1)) for a, b in zip(CH_A, CH_B)]
LN_EPS = 1e-5
SM_SCALE = 0.125

DBG = {}
_CACHE = {}


def build_nc():
    nc = bass.Bass("TRN2", target_bir_lowering=False)

    x_q = nc.dram_tensor("x_q", [TOK, D], F32, kind="ExternalInput")
    x_f = nc.dram_tensor("x_f", [S, D], F32, kind="ExternalInput")
    masks = nc.dram_tensor("masks", [NSLOT * 4 * 128, CHK], BF16, kind="ExternalInput")
    wqkvT = nc.dram_tensor("wqkvT", [D, 3 * D], F32, kind="ExternalInput")
    qkv_bias = nc.dram_tensor("qkv_bias", [128, 24], F32, kind="ExternalInput")
    sh_bias = nc.dram_tensor("sh_bias", [2 * 128, D], F32, kind="ExternalInput")
    wprojT = nc.dram_tensor("wprojT", [D, D], F32, kind="ExternalInput")
    gate_wT = nc.dram_tensor("gate_wT", [D, NE], F32, kind="ExternalInput")
    egate_b = nc.dram_tensor("egate_b", [128, NE], F32, kind="ExternalInput")
    wg_all = nc.dram_tensor("wg_all", [NEX, D, D], F32, kind="ExternalInput")
    wu_all = nc.dram_tensor("wu_all", [NEX, D, D], F32, kind="ExternalInput")
    wd_all = nc.dram_tensor("wd_all", [NEX, D, D], F32, kind="ExternalInput")
    y = nc.dram_tensor("y", [TOK, D], F32, kind="ExternalOutput")

    spk = "ExternalOutput" if DBG.get("spills") else "Internal"
    kspill = nc.dram_tensor("kspill", [8, 128, S], BF16, kind=spk)
    vspill = nc.dram_tensor("vspill", [NH, 32, 128, 65], BF16, kind=spk)
    aotspill = nc.dram_tensor("aotspill", [8, 128, TOK], BF16, kind=spk)
    dbg_t = {}
    for name, shape, dt_ in (("qt_dbg", [128, 16384], BF16),
                             ("xqt_dbg", [128, 16384], BF16),
                             ("xh_dbg", [128, 16 * D], BF16),
                             ("out1_dbg", [128, 16 * D], F32),
                             ("x2t_dbg", [128, 8 * TOK], BF16),
                             ("g_dbg", [128, 16 * NEX], F32)):
        if DBG.get(name):
            dbg_t[name] = nc.dram_tensor(name, shape, dt_, kind="ExternalOutput")

    BASE = 16512
    KB = 1024

    def A(name, shape, dt_, off_b):
        assert (BASE + off_b) % 32 == 0, name
        return nc.alloc_sbuf_tensor_at(name, shape, dt_, offset=BASE + off_b)

    xqt = A("xqt", [128, 8 * TOK], BF16, 0)
    xst1 = A("xst1", [128, 2 * D], BF16, 32 * KB)
    xft = A("xft", [128, 8 * S], BF16, 0)
    vbuf = A("vbuf", [128, 2 * 4160], BF16, 0)
    acc = A("acc", [128, 16 * D], F32, 0)
    qt = A("qt", [128, 8 * TOK], BF16, 64 * KB)
    x2st = A("x2st", [128, 2 * D], BF16, 64 * KB)
    pstage = A("pstage", [128, 2 * D], BF16, 68 * KB)
    rtk = A("rtk", [128, 6 * 256], F32, 72 * KB)
    hbuf = A("hbuf", [128, 8 * TOK], BF16, 64 * KB)
    x2t = A("x2t", [128, 8 * TOK], BF16, 96 * KB)
    u0 = A("u0", [128, 8 * D], BF16, 128 * KB)
    u1 = A("u1", [128, 8 * D], BF16, 144 * KB)
    u2 = A("u2", [128, 8 * D], BF16, 160 * KB)
    u3 = A("u3", [128, 8 * D], BF16, 176 * KB)
    U = [u0, u1, u2, u3]
    wqkv_sb = u0
    kevac = A("kevac", [128, 2 * S], BF16, 144 * KB)
    masks_sb = u2
    vstage = A("vstage", [128, 2 * 1040], BF16, 176 * KB)
    xst2 = A("xst2", [128, 2 * D], BF16, 176 * KB + 4160)
    kbuf = A("kbuf", [128, 2 * S], BF16, 176 * KB)
    wproj_sb = u3
    xst = A("xst", [128, 2 * D], F32, 192 * KB)
    exps_a = A("exps_a", [128, 4 * CHK], BF16, 192 * KB)
    exps_b = A("exps_b", [128, 4 * CHK], BF16, 194 * KB)
    tmpo = A("tmpo", [128, CHK], F32, 196 * KB)
    tmpf = A("tmpf", [128, 2 * CHK], BF16, 197 * KB)
    rz_sb = A("rz_sb", [1, 2 * CHK], F32, 198 * KB)
    lnst = A("lnst", [128, 6 * 32], F32, int(184.5 * KB))
    lnst4 = A("lnst4", [128, 6 * 32], F32, 78 * KB)
    silug = A("silug", [128, 2048], BF16, 192 * KB)
    mis = [200 * KB]

    def M(name, shape, dt_):
        t = A(name, shape, dt_, mis[0])
        nby = int(np.prod(shape[1:])) * (2 if dt_ == BF16 else 4)
        mis[0] += (nby + 31) // 32 * 32
        return t

    shb_sb = M("shb_sb", [128, D], F32)
    qkvb_sb = M("qkvb_sb", [128, 24], F32)
    egate_sb = M("egate_sb", [128, NE], F32)
    gatew_sb = M("gatew_sb", [128, 8 * NE], BF16)
    g_sb = M("g_sb", [128, 16 * NEX], F32)
    lnc = M("lnc", [128, 8], F32)
    tk = M("tk", [128, 4 * NE], F32)
    ones64 = M("ones64", [1, 64], F32)
    bigp = M("bigp", [128, NE], F32)
    mcol = M("mcol", [128, 4], F32)
    assert mis[0] <= int(207.8 * KB), mis[0]

    pA = nc.psum_tensor("pA", [128, 1024], F32).__enter__()
    pB = nc.psum_tensor("pB", [128, 1024], F32).__enter__()
    p4 = nc.psum_tensor("p4", [128, 512], F32).__enter__()
    p5 = nc.psum_tensor("p5", [128, 512], F32).__enter__()
    p6 = nc.psum_tensor("p6", [128, 512], F32).__enter__()
    p7 = nc.psum_tensor("p7", [128, 512], F32).__enter__()
    P512 = [p4, p5, p6, p7]

    names = ["s_x", "s_w", "s_tr", "s_sp", "s_kv", "s_mm", "s_sc", "s_ve",
             "s_out", "s_uf0", "s_uf1", "s_uf2", "s_uf3"]
    SEM = {n: nc.semaphore(n).__enter__() for n in names}
    CNT = {n: 0 for n in names}

    te, sc, ve, sy, gp = nc.tensor, nc.scalar, nc.vector, nc.sync, nc.gpsimd

    def dma(eng, dst, src, sem):
        eng.dma_start(dst, src).then_inc(SEM[sem], 16)
        CNT[sem] += 16
        return CNT[sem]

    def dmat(eng, dst, src):
        eng.dma_start_transpose(dst, src).then_inc(SEM["s_tr"], 16)
        CNT["s_tr"] += 16
        return CNT["s_tr"]

    def inc(inst_or_eng, sem, via_sem_inc=False):
        if via_sem_inc:
            inst_or_eng.sem_inc(SEM[sem], 1)
        else:
            inst_or_eng.then_inc(SEM[sem], 1)
        CNT[sem] += 1
        return CNT[sem]

    def w(eng, sem, val):
        if val > 0:
            eng.wait_ge(SEM[sem], val)

    plan = {}

    p0_done = dma(sy, qkvb_sb[:], qkv_bias[:, :], "s_x")
    dma(gp, wqkv_sb[:].rearrange("p (i c) -> p i c", c=D),
        wqkvT[:, 0:D].rearrange("(i p) c -> p i c", p=128), "s_w")
    plan["wq0"] = CNT["s_w"]
    dma(gp, gatew_sb[:].rearrange("p (i c) -> p i c", c=NE),
        gate_wT.rearrange("(i p) c -> p i c", p=128), "s_w")
    plan["wgate"] = CNT["s_w"]
    plan["wq"] = CNT["s_w"]
    ve.memset(ones64[:], 1.0)
    ve.memset(bigp[:], -1e30)
    ve.memset(mcol[:, 3:4], LN_EPS)
    ve.memset(g_sb[:].rearrange("p (t e) -> p t e", e=NEX)[:, :, NE:NEX], 1.0)
    inc(ve.memset(vstage[:].rearrange("p (b h c) -> p b h c", h=NH, c=65)[:, :, :, 64:65], 1.0),
        "s_ve")
    if DBG.get("phase") == 0:
        w(sy, "s_x", CNT["s_x"])
        w(sy, "s_w", CNT["s_w"])
        w(sy, "s_ve", CNT["s_ve"])
        return nc

    ln_state = {"prev": 0}

    def ln_stats(xin, st, col, xv):
        w(sc, "s_ve", ln_state["prev"])
        i1 = inc(sc.activation(pA[:, 0:1024], xin, AF.Square), "s_sc")
        w(ve, "s_x", xv)
        inc(ve.reduce_sum(st[:, col:col + 1], xin, axis=AX.X), "s_ve")
        w(ve, "s_sc", i1)
        ln_state["prev"] = inc(
            ve.reduce_sum(st[:, 32 + col:33 + col], pA[:, 0:1024], axis=AX.X), "s_ve")

    def ln_chain(st, n):
        s1, s2 = st[:, 0:n], st[:, 32:32 + n]
        c4, c5 = st[:, 64:64 + n], st[:, 96:96 + n]
        rstd, nb = st[:, 128:128 + n], st[:, 160:160 + n]
        ve.wait_ge(SEM["s_ve"], ln_state["prev"])
        i1 = inc(ve.tensor_scalar(c4, s2, 1.0 / D, LN_EPS, ALU.mult, ALU.add), "s_ve")
        w(sc, "s_ve", i1)
        i2 = inc(sc.activation(c5, s1, AF.Square), "s_sc")
        w(ve, "s_sc", i2)
        i3 = inc(ve.scalar_tensor_tensor(c5, c5, -1.0 / (D * D), c4,
                                         ALU.mult, ALU.add), "s_ve")
        ve.wait_ge(SEM["s_ve"], i3)
        i4 = inc(ve.reciprocal(c4, c5), "s_ve")
        w(sc, "s_ve", i4)
        i5 = inc(sc.activation(rstd, c4, AF.Sqrt), "s_sc")
        w(ve, "s_sc", i5)
        i6 = inc(ve.tensor_tensor(nb, s1, rstd, ALU.mult), "s_ve")
        ve.wait_ge(SEM["s_ve"], i6)
        return inc(ve.tensor_scalar(nb, nb, -1.0 / D, None, ALU.mult), "s_ve")

    def ln_norm(xin, xout, st, col, chain_v):
        w(sc, "s_ve", chain_v)
        return inc(sc.activation(xout, xin, AF.Identity,
                                 bias=st[:, 160 + col:161 + col],
                                 scale=st[:, 128 + col:129 + col]), "s_sc")

    def emit_ln(xin, xout, xv=0):
        w(sc, "s_ve", ln_state["prev"])
        i1 = inc(sc.activation(pA[:, 0:1024], xin, AF.Square), "s_sc")
        w(ve, "s_x", xv)
        i0 = inc(ve.reduce_sum(lnc[:, 0:1], xin, axis=AX.X), "s_ve")
        w(ve, "s_sc", i1)
        iprev = inc(ve.reduce_sum(lnc[:, 1:2], pA[:, 0:1024], axis=AX.X), "s_ve")
        ln_state["prev"] = iprev
        ve.wait_ge(SEM["s_ve"], iprev)
        i2v = inc(ve.tensor_scalar(lnc[:, 4:5], lnc[:, 1:2], 1.0 / D, LN_EPS,
                                   ALU.mult, ALU.add), "s_ve")
        w(sc, "s_ve", i2v)
        sc.mul(lnc[:, 2:3], lnc[:, 0:1], lnc[:, 0:1])
        i2 = inc(sc.activation(lnc[:, 5:6], lnc[:, 2:3], AF.Identity,
                               bias=lnc[:, 4:5], scale=-1.0 / (D * D)), "s_sc")
        w(ve, "s_sc", i2)
        i3 = inc(ve.reciprocal(lnc[:, 6:7], lnc[:, 5:6]), "s_ve")
        w(sc, "s_ve", i3)
        i4 = inc(sc.activation(lnc[:, 7:8], lnc[:, 6:7], AF.Sqrt), "s_sc")
        w(ve, "s_sc", i4)
        i5 = inc(ve.tensor_scalar(lnc[:, 2:3], lnc[:, 7:8], lnc[:, 0:1],
                                  -1.0 / D, ALU.mult, ALU.mult), "s_ve")
        w(sc, "s_ve", i5)
        return inc(sc.activation(xout, xin, AF.Identity, bias=lnc[:, 2:3],
                                 scale=lnc[:, 7:8]), "s_sc")

    for tb in range(2):
        plan[("xq", tb)] = dma(sy, xst[:, (tb % 2) * D:(tb % 2 + 1) * D],
                               x_q[tb * 128:(tb + 1) * 128, :], "s_x")
    for tb in range(16):
        xin = xst[:, (tb % 2) * D:(tb % 2 + 1) * D]
        w(sc, "s_x", plan[("xq", tb)])
        ln_stats(xin, lnst, tb, plan[("xq", tb)])
        if tb + 2 < 16:
            w(sy, "s_ve", ln_state["prev"])
            plan[("xq", tb + 2)] = dma(sy, xst[:, (tb % 2) * D:(tb % 2 + 1) * D],
                                       x_q[(tb + 2) * 128:(tb + 3) * 128, :], "s_x")
    chv = ln_chain(lnst, 16)
    dma(sy, masks_sb[:].rearrange("p (s c) -> p s c", c=CHK),
        masks.rearrange("(s p) c -> p s c", p=128), "s_x")
    dma(sy, egate_sb[:], egate_b[:, :], "s_x")
    p0_done = dma(sy, shb_sb[:], sh_bias[0:128, :], "s_x")
    w(sy, "s_ve", ln_state["prev"])
    for tb in range(2):
        plan[("xqc", tb)] = dma(sy, xst[:, (tb % 2) * D:(tb % 2 + 1) * D],
                                x_q[tb * 128:(tb + 1) * 128, :], "s_x")
    tr_q = []
    for tb in range(16):
        xin = xst[:, (tb % 2) * D:(tb % 2 + 1) * D]
        xout = xst1[:, (tb % 2) * D:(tb % 2 + 1) * D]
        w(sc, "s_x", plan[("xqc", tb)])
        if tb >= 2:
            w(sc, "s_tr", tr_q[tb - 2])
        nrm = ln_norm(xin, xout, lnst, tb, chv)
        w(sc, "s_sc", nrm)
        w(sy, "s_sc", nrm)
        for i in range(8):
            dst = xqt[:, 2048 * i + 128 * tb: 2048 * i + 128 * (tb + 1)]
            src = xout[:, 128 * i:128 * (i + 1)]
            dmat(sc if i % 2 == 0 else sy, dst, src)
        tr_q.append(CNT["s_tr"])
        if tb + 2 < 16:
            w(sy, "s_sc", nrm)
            plan[("xqc", tb + 2)] = dma(sy, xst[:, (tb % 2) * D:(tb % 2 + 1) * D],
                                        x_q[(tb + 2) * 128:(tb + 3) * 128, :], "s_x")
    w(te, "s_w", plan["wq0"])
    w(sc, "s_x", p0_done)
    qev = {}
    for tt in range(4):
        w(te, "s_tr", tr_q[4 * tt + 3])
        for ob in range(8):
            pt = P512[ob % 4]
            key = (tt, ob - 4) if ob >= 4 else (tt - 1, ob + 4)
            if key in qev:
                w(te, "s_sc", qev[key])
            for i in range(8):
                mm = te.matmul(pt[:, :],
                               wqkv_sb[:, D * i + 128 * ob: D * i + 128 * (ob + 1)],
                               xqt[:, 2048 * i + 512 * tt: 2048 * i + 512 * (tt + 1)],
                               start=(i == 0), stop=(i == 7))
            mmv = inc(mm, "s_mm")
            w(sc, "s_mm", mmv)
            qev[(tt, ob)] = inc(
                sc.activation(qt[:, 2048 * ob + 512 * tt: 2048 * ob + 512 * (tt + 1)],
                              pt[:, :], AF.Identity, bias=qkvb_sb[:, ob:ob + 1]), "s_sc")
    plan["qdone_mm"] = CNT["s_mm"]
    plan["qdone_ev"] = CNT["s_sc"]
    if DBG.get("qt_dbg"):
        w(sy, "s_sc", plan["qdone_ev"])
        dma(sy, dbg_t["qt_dbg"][:, :], qt[:], "s_out")
    if DBG.get("xqt_dbg"):
        w(sy, "s_tr", tr_q[15])
        dma(sy, dbg_t["xqt_dbg"][:, :], xqt[:], "s_out")
    if DBG.get("xh_dbg"):
        w(sy, "s_ve", CNT["s_ve"])
        dma(sy, dbg_t["xh_dbg"][:, :], x2t[:, 0:16 * D], "s_out")
    if DBG.get("phase") == 1:
        sy.wait_ge(SEM["s_out"], CNT["s_out"])
        return nc

    w(gp, "s_mm", plan["qdone_mm"])
    plan["wk"] = dma(gp, wqkv_sb[:].rearrange("p (i c) -> p i c", c=D),
                     wqkvT[:, D:2 * D].rearrange("(i p) c -> p i c", p=128), "s_w")
    for tb in range(2):
        if tb == 0:
            w(sy, "s_sc", plan["qdone_ev"])
        plan[("xf", tb)] = dma(sy, xst[:, (tb % 2) * D:(tb % 2 + 1) * D],
                               x_f[tb * 128:(tb + 1) * 128, :], "s_x")
    for tb in range(32):
        xin = xst[:, (tb % 2) * D:(tb % 2 + 1) * D]
        w(sc, "s_x", plan[("xf", tb)])
        ln_stats(xin, lnst, tb, plan[("xf", tb)])
        if tb + 2 < 32:
            w(sy, "s_ve", ln_state["prev"])
            plan[("xf", tb + 2)] = dma(sy, xst[:, (tb % 2) * D:(tb % 2 + 1) * D],
                                       x_f[(tb + 2) * 128:(tb + 3) * 128, :], "s_x")
    chv = ln_chain(lnst, 32)
    w(sy, "s_ve", ln_state["prev"])
    for tb in range(2):
        plan[("xfc", tb)] = dma(sy, xst[:, (tb % 2) * D:(tb % 2 + 1) * D],
                                x_f[tb * 128:(tb + 1) * 128, :], "s_x")
    tr_f = []
    for tb in range(32):
        xin = xst[:, (tb % 2) * D:(tb % 2 + 1) * D]
        xout = xst2[:, (tb % 2) * D:(tb % 2 + 1) * D]
        w(sc, "s_x", plan[("xfc", tb)])
        if tb == 0:
            w(sc, "s_mm", plan["qdone_mm"])
        if tb >= 2:
            w(sc, "s_tr", tr_f[tb - 2])
        nrm = ln_norm(xin, xout, lnst, tb, chv)
        w(sc, "s_sc", nrm)
        w(sy, "s_sc", nrm)
        for i in range(8):
            dst = xft[:, 4096 * i + 128 * tb: 4096 * i + 128 * (tb + 1)]
            src = xout[:, 128 * i:128 * (i + 1)]
            dmat(sc if i % 2 == 0 else sy, dst, src)
        tr_f.append(CNT["s_tr"])
        if tb + 2 < 32:
            w(sy, "s_sc", nrm)
            plan[("xfc", tb + 2)] = dma(sy, xst[:, (tb % 2) * D:(tb % 2 + 1) * D],
                                        x_f[(tb + 2) * 128:(tb + 3) * 128, :], "s_x")
    w(te, "s_w", plan["wk"])
    kev = {}
    ksp = {}
    for ob in range(8):
        for tt in range(8):
            w(te, "s_tr", tr_f[4 * tt + 3])
            idx = 8 * ob + tt
            pt = P512[idx % 4]
            if idx >= 4:
                pk = (idx - 4) // 8, (idx - 4) % 8
                w(te, "s_sc", kev[pk])
            for i in range(8):
                mm = te.matmul(pt[:, :],
                               wqkv_sb[:, D * i + 128 * ob: D * i + 128 * (ob + 1)],
                               xft[:, 4096 * i + 512 * tt: 4096 * i + 512 * (tt + 1)],
                               start=(i == 0), stop=(i == 7))
            mmv = inc(mm, "s_mm")
            w(sc, "s_mm", mmv)
            if tt == 0 and ob >= 2:
                w(sc, "s_sp", ksp[ob - 2])
            kev[(ob, tt)] = inc(
                sc.activation(kevac[:, (ob % 2) * S + 512 * tt:(ob % 2) * S + 512 * (tt + 1)],
                              pt[:, :], AF.Identity, bias=qkvb_sb[:, 8 + ob:9 + ob]), "s_sc")
        w(sy, "s_sc", kev[(ob, 7)])
        ksp[ob] = dma(sy, kspill[ob, :, :], kevac[:, (ob % 2) * S:(ob % 2 + 1) * S], "s_sp")
    plan["kdone_mm"] = CNT["s_mm"]
    w(gp, "s_mm", plan["kdone_mm"])
    plan["wv"] = dma(gp, wqkv_sb[:].rearrange("p (i c) -> p i c", c=D),
                     wqkvT[:, 2 * D:3 * D].rearrange("(i p) c -> p i c", p=128), "s_w")
    w(te, "s_w", plan["wv"])
    w(ve, "s_x", p0_done)
    vev = {}
    vsp = {}
    for jj in range(32):
        for vfb in range(2):
            idx = 2 * jj + vfb
            pt = P512[idx % 4]
            if idx >= 4:
                w(te, "s_ve", vev[((idx - 4) // 2, (idx - 4) % 2)])
            for i in range(8):
                mm = te.matmul(pt[:, :],
                               xft[:, 4096 * i + 128 * jj: 4096 * i + 128 * (jj + 1)],
                               wqkv_sb[:, D * i + 512 * vfb: D * i + 512 * (vfb + 1)],
                               start=(i == 0), stop=(i == 7))
            mmv = inc(mm, "s_mm")
            w(ve, "s_mm", mmv)
            if vfb == 0 and jj >= 2:
                w(ve, "s_sp", vsp[jj - 2])
            dst = vstage[:, (jj % 2) * 1040 + 520 * vfb:(jj % 2) * 1040 + 520 * (vfb + 1)]
            vev[(jj, vfb)] = inc(
                ve.tensor_tensor(dst.rearrange("p (h c) -> p h c", c=65)[:, :, 0:64],
                                 pt[:].rearrange("p (h c) -> p h c", c=64),
                                 shb_sb[:, 512 * vfb:512 * (vfb + 1)]
                                 .rearrange("p (h c) -> p h c", c=64), ALU.add), "s_ve")
        w(sy, "s_ve", vev[(jj, 1)])
        vsp[jj] = dma(sy, vspill[:, jj, :, :].rearrange("h p c -> p h c"),
                      vstage[:, (jj % 2) * 1040:(jj % 2 + 1) * 1040]
                      .rearrange("p (h c) -> p h c", c=65), "s_sp")
    plan["vdone_mm"] = CNT["s_mm"]
    plan["vdone_sp"] = CNT["s_sp"]

    w(gp, "s_mm", plan["vdone_mm"])
    w(gp, "s_sp", ksp[7])
    plan[("wg", 0)] = dma(gp, u0[:].rearrange("p (i c) -> p i c", c=D),
                          wg_all[0].rearrange("(i p) c -> p i c", p=128), "s_w")
    plan[("wu", 0)] = dma(gp, u1[:].rearrange("p (i c) -> p i c", c=D),
                          wu_all[0].rearrange("(i p) c -> p i c", p=128), "s_w")
    if DBG.get("phase") == 2:
        w(sy, "s_sp", CNT["s_sp"])
        sy.wait_ge(SEM["s_out"], CNT["s_out"])
        return nc

    kv_ready = {}

    def load_pair(j):
        if j < 2:
            w(sy, "s_sp", plan["vdone_sp"])
            w(sy, "s_tr", tr_f[31])
        else:
            w(sy, "s_mm", plan[("pair_mm_done", j - 2)])
        w(sy, "s_sp", ksp[j])
        dma(sy, kbuf[:, (j % 2) * S:(j % 2 + 1) * S], kspill[j, :, :], "s_kv")
        if j < 2:
            w(sy, "s_mm", plan["vdone_mm"])
        dma(sy, vbuf[:, (j % 2) * 4160:(j % 2) * 4160 + 2080]
            .rearrange("p (jj c) -> p jj c", c=65),
            vspill[2 * j].rearrange("jj p c -> p jj c"), "s_kv")
        kv_ready[j] = dma(sy, vbuf[:, (j % 2) * 4160 + 2080:(j % 2) * 4160 + 4160]
                          .rearrange("p (jj c) -> p jj c", c=65),
                          vspill[2 * j + 1].rearrange("jj p c -> p jj c"), "s_kv")

    w(ve, "s_x", p0_done)
    load_pair(0)
    load_pair(1)
    fin_prev = {}
    av_bank_free = {0: 0, 1: 0}
    zb_free = 0
    exp_done = {}
    gctr = [0]
    aotsp = []
    for j in range(8):
        w(te, "s_ve", CNT["s_ve"])
        w(te, "s_sc", CNT["s_sc"])
        kof = (j % 2) * S
        vof = (j % 2) * 4160
        w(te, "s_kv", kv_ready[j])
        for s in range(NSLOT):
            skt = SLOT_KT[s]
            ng = skt // 2
            for g in range(ng):
                glob = gctr[0]; gctr[0] += 1
                sl = glob % 2
                smm = {}
                for par, (pS, kb0) in enumerate(((pA, 0), (pB, 64))):
                    pv = exp_done.get((par, glob - 2))
                    if pv:
                        w(te, "s_sc", pv)
                    for q in range(2):
                        kt = 2 * g + q
                        mm = te.matmul(pS[:, 512 * sl + 256 * q: 512 * sl + 256 * (q + 1)],
                                       kbuf[kb0:kb0 + 64, kof + 128 * kt: kof + 128 * (kt + 1)],
                                       qt[kb0:kb0 + 64, 2048 * j + 256 * s: 2048 * j + 256 * (s + 1)],
                                       start=True, stop=True)
                    smm[par] = inc(mm, "s_mm")
                for par in range(2):
                    pS = pA if par == 0 else pB
                    ex = exps_a if par == 0 else exps_b
                    exs = ex[:, 512 * sl:512 * (sl + 1)]
                    w(sc, "s_mm", smm[1])
                    pv = plan.get(("avdone", par, glob - 2))
                    if pv:
                        w(sc, "s_mm", pv)
                    exv = inc(sc.activation(exs, pS[:, 512 * sl:512 * (sl + 1)],
                                            AF.Exp, scale=SM_SCALE), "s_sc")
                    exp_done[(par, glob)] = exv
                    gate_sem, gate_val = "s_sc", exv
                    if g >= ng - 2:
                        w(ve, "s_sc", exv)
                        mks = masks_sb[:, 1024 * s + 512 * (g - (ng - 2)):
                                       1024 * s + 512 * (g - (ng - 2) + 1)]
                        gate_val = inc(ve.tensor_tensor(exs, exs, mks, ALU.mult), "s_ve")
                        gate_sem = "s_ve"
                    w(te, gate_sem, gate_val)
                    pO = p4 if par == 0 else p5
                    if g == 0 and av_bank_free[par]:
                        w(te, "s_ve", av_bank_free[par])
                    for q in range(2):
                        kt = 2 * g + q
                        mm = te.matmul(pO[0:65, 0:256],
                                       vbuf[:, vof + 2080 * par + 65 * kt: vof + 2080 * par + 65 * (kt + 1)],
                                       exs[:, 256 * q:256 * (q + 1)],
                                       start=(kt == 0), stop=(kt == skt - 1))
                    plan[("avdone", par, glob)] = inc(mm, "s_mm")
                    plan[("avdone", par)] = plan[("avdone", par, glob)]
            for par in range(2):
                pO = p4 if par == 0 else p5
                w(ve, "s_mm", plan[("avdone", par)])
                rz = rz_sb[0:1, 256 * par:256 * (par + 1)]
                cv = inc(ve.tensor_copy(tmpo[0:64, 0:256], pO[0:64, 0:256]), "s_ve")
                rv = inc(ve.reciprocal(rz, pO[64:65, 0:256]), "s_ve")
                av_bank_free[par] = rv
                w(te, "s_ve", rv)
                if zb_free:
                    w(te, "s_ve", zb_free)
                zbv = inc(te.matmul(p6[0:64, 0:256], ones64[0:1, 0:64], rz,
                                    start=True, stop=True), "s_mm")
                w(ve, "s_mm", zbv)
                if len(aotsp) >= 2:
                    w(ve, "s_sp", aotsp[-2])
                fv = inc(ve.tensor_tensor(tmpf[0:64, 256 * par:256 * (par + 1)],
                                          tmpo[0:64, 0:256], p6[0:64, 0:256],
                                          ALU.mult), "s_ve")
                zb_free = fv
                w(sy, "s_ve", fv)
                aotsp.append(dma(sy, aotspill[j, 64 * par:64 * (par + 1), 256 * s:256 * (s + 1)],
                                 tmpf[0:64, 256 * par:256 * (par + 1)], "s_sp"))
        plan[("pair_mm_done", j)] = CNT["s_mm"]
        if j + 2 < 8:
            load_pair(j + 2)
    plan["attn_mm_done"] = CNT["s_mm"]
    plan["attn_ve_done"] = CNT["s_ve"]
    plan["attn_sp_done"] = CNT["s_sp"]
    if DBG.get("phase") == 3:
        w(sy, "s_sp", CNT["s_sp"])
        sy.wait_ge(SEM["s_out"], CNT["s_out"])
        return nc

    w(gp, "s_mm", plan["attn_mm_done"])
    plan["wproj"] = dma(gp, wproj_sb[:].rearrange("p (i c) -> p i c", c=D),
                        wprojT.rearrange("(i p) c -> p i c", p=128), "s_w")
    w(sy, "s_ve", plan["attn_ve_done"])
    plan["pbias"] = dma(sy, shb_sb[:], sh_bias[128:256, :], "s_x")
    pst = {}

    def p4_loads(tb):
        if tb == 0:
            w(sy, "s_sp", plan["attn_sp_done"])
            w(sy, "s_ve", plan["attn_ve_done"])
            w(sy, "s_sc", CNT["s_sc"])
        if tb >= 2:
            w(sy, "s_mm", plan[("projmm", tb - 2)])
            w(sy, "s_ve", plan[("acc_ev", tb - 2)])
        pst[tb] = dma(sy, pstage[:, (tb % 2) * D:(tb % 2 + 1) * D]
                      .rearrange("p (ab c) -> p ab c", c=128),
                      aotspill[:, :, 128 * tb:128 * (tb + 1)]
                      .rearrange("ab p c -> p ab c"), "s_x")
        plan[("xq2", tb)] = dma(sy, xst[:, (tb % 2) * D:(tb % 2 + 1) * D],
                                x_q[tb * 128:(tb + 1) * 128, :], "s_x")

    p4_loads(0)
    p4_loads(1)
    w(te, "s_w", plan["wproj"])
    prev_ev = {}
    tr2 = []
    gdone = {}
    for tb in range(16):
        w(te, "s_x", pst[tb])
        for of in range(2):
            pt = p4 if of == 0 else p5
            if tb > 0:
                w(te, "s_ve", prev_ev[(tb - 1, of)])
            for ab in range(8):
                mm = te.matmul(pt[:, :],
                               pstage[:, (tb % 2) * D + 128 * ab:(tb % 2) * D + 128 * (ab + 1)],
                               wproj_sb[:, D * ab + 512 * of: D * ab + 512 * (of + 1)],
                               start=(ab == 0), stop=(ab == 7))
            mmv = inc(mm, "s_mm")
            plan[("projmm", tb)] = CNT["s_mm"]
            w(ve, "s_mm", mmv)
            w(ve, "s_x", max(plan[("xq2", tb)], plan["pbias"]))
            aslc = acc[:, 1024 * tb + 512 * of: 1024 * tb + 512 * (of + 1)]
            ve.tensor_tensor(aslc, pt[:, :], shb_sb[:, 512 * of:512 * (of + 1)], ALU.add)
            prev_ev[(tb, of)] = inc(
                ve.tensor_tensor(aslc, aslc,
                                 xst[:, (tb % 2) * D + 512 * of:(tb % 2) * D + 512 * (of + 1)],
                                 ALU.add), "s_ve")
            plan[("acc_ev", tb)] = prev_ev[(tb, of)]
            if of == 1 and tb + 2 < 16:
                p4_loads(tb + 2)
        w(sc, "s_ve", prev_ev[(tb, 1)])
        ln_stats(acc[:, 1024 * tb:1024 * (tb + 1)], lnst4, tb, 0)
    chv = ln_chain(lnst4, 16)
    for tb in range(16):
        if tb == 0:
            w(sc, "s_mm", plan["attn_mm_done"])
        if tb >= 2:
            w(sc, "s_tr", tr2[tb - 2])
        xout = x2st[:, (tb % 2) * D:(tb % 2 + 1) * D]
        nrm = ln_norm(acc[:, 1024 * tb:1024 * (tb + 1)], xout, lnst4, tb, chv)
        w(sc, "s_sc", nrm)
        w(sy, "s_sc", nrm)
        for i in range(8):
            dst = x2t[:, 2048 * i + 128 * tb: 2048 * i + 128 * (tb + 1)]
            src = xout[:, 128 * i:128 * (i + 1)]
            dmat(sc if i % 2 == 0 else sy, dst, src)
        tr2.append(CNT["s_tr"])
        w(te, "s_tr", tr2[tb])
        w(te, "s_w", plan["wgate"])
        for i in range(8):
            mm = te.matmul(p7[:, NE * tb:NE * (tb + 1)],
                           x2t[:, 2048 * i + 128 * tb: 2048 * i + 128 * (tb + 1)],
                           gatew_sb[:, NE * i:NE * (i + 1)],
                           start=(i == 0), stop=(i == 7))
        mmv = inc(mm, "s_mm")
        plan[("rtmm", tb)] = mmv
    def bc(ap16):
        return bass.AP(ap16.tensor, ap16.offset,
                       [ap16.ap[0], [0, 16], ap16.ap[1]])

    def bce(ap16):
        return bass.AP(ap16.tensor, ap16.offset,
                       [ap16.ap[0], ap16.ap[1], [0, 16]])

    def vbar(v_inst):
        val = inc(v_inst, "s_ve")
        ve.wait_ge(SEM["s_ve"], val)
        return val

    g3 = lambda t: t.rearrange("p (g c) -> p g c", c=NE)
    lg, bi = rtk[:, 0:256], rtk[:, 256:512]
    wka, wkb = rtk[:, 512:768], rtk[:, 768:1024]
    scr, u = rtk[:, 1024:1280], rtk[:, 1280:1536]
    mred = mcol_r = rtk[:, 1280:1296]
    w(ve, "s_mm", plan[("rtmm", 15)])
    vbar(ve.tensor_copy(lg, p7[:, 0:256]))
    vbar(ve.tensor_tensor(g3(bi), g3(lg), bc(egate_sb[:, 0:NE]), ALU.add))
    vbar(ve.tensor_copy(wka, bi))
    cur, nxt = wka, wkb
    for r in range(4):
        vbar(ve.reduce_max(tk[:, 0:NE].rearrange("p (g one) -> p g one", one=1),
                           g3(cur), axis=AX.X))
        if r < 3:
            vbar(ve.tensor_tensor(g3(scr), g3(cur), bce(tk[:, 0:NE]), ALU.is_lt))
            vbar(ve.tensor_tensor(nxt, cur, scr, ALU.mult))
            vbar(ve.scalar_tensor_tensor(g3(u), g3(scr), 1e30, bc(bigp[:, 0:NE]),
                                         ALU.mult, ALU.add))
            vbar(ve.tensor_tensor(nxt, nxt, u, ALU.add))
            cur, nxt = nxt, cur
    vbar(ve.tensor_tensor(g3(scr), g3(bi), bce(tk[:, 0:NE]), ALU.is_lt))
    sel = vbar(ve.tensor_scalar(scr, scr, -1.0, 1.0, ALU.mult, ALU.add))
    w(sc, "s_ve", sel)
    epv = inc(sc.activation(lg, lg, AF.Exp), "s_sc")
    w(ve, "s_sc", epv)
    vbar(ve.tensor_tensor(wka, lg, scr, ALU.mult))
    vbar(ve.reduce_sum(tk[:, NE:2 * NE].rearrange("p (g one) -> p g one", one=1),
                       g3(wka), axis=AX.X))
    vbar(ve.reciprocal(tk[:, 2 * NE:3 * NE], tk[:, NE:2 * NE]))
    gdone = vbar(ve.tensor_tensor(
        g_sb[:].rearrange("p (t e) -> p t e", e=NEX)[:, :, 0:NE],
        g3(wka), bce(tk[:, 2 * NE:3 * NE]), ALU.mult))
    plan["p4_mm_done"] = CNT["s_mm"]
    plan["p4_ve_done"] = CNT["s_ve"]
    plan["p4_tr_done"] = CNT["s_tr"]

    if DBG.get("out1_dbg"):
        w(sy, "s_ve", plan["p4_ve_done"])
        w(sy, "s_tr", plan["p4_tr_done"])
        dma(sy, dbg_t["out1_dbg"][:, :], acc[:], "s_out")
        dma(sy, dbg_t["x2t_dbg"][:, :], x2t[:], "s_out")
        dma(sy, dbg_t["g_dbg"][:, :], g_sb[:], "s_out")

    if DBG.get("phase") == 4:
        sy.wait_ge(SEM["s_out"], CNT["s_out"])
        return nc
    w(gp, "s_ve", plan["attn_ve_done"])
    plan[("wd", 0)] = dma(gp, u2[:].rearrange("p (i c) -> p i c", c=D),
                          wd_all[0].rearrange("(i p) c -> p i c", p=128), "s_w")

    uf_cnt = {0: 0, 1: 0, 2: 0, 3: plan["p4_mm_done"]}
    wkey = {0: (plan[("wg", 0)], plan[("wu", 0)], plan[("wd", 0)])}
    last_acc = {tb: plan["p4_ve_done"] for tb in range(16)}
    silu_done = {}
    h_done = {}
    out_cnt = 0
    for e in range(NEX):
        un_g, un_u, un_d = (3 * e) % 4, (3 * e + 1) % 4, (3 * e + 2) % 4
        if e > 0:
            wg_v = plan[("wx", e, 0)]
            wu_v = plan[("wx", e, 1)]
            wd_v = plan[("wx", e, 2)]
        else:
            wg_v, wu_v, wd_v = wkey[0]
        wg_sb, wu_sb, wd_sb = U[un_g], U[un_u], U[un_d]
        w(te, "s_w", wg_v)
        if e == 0:
            w(te, "s_tr", plan["p4_tr_done"])
            w(te, "s_ve", plan["p4_ve_done"])
        for ib in range(8):
            for i in range(8):
                for tq in range(4):
                    pt = P512[tq]
                    if i == 0:
                        if ib == 0:
                            pv = plan.get(("dnlast", e - 1, 14 + tq // 2, tq % 2))
                        else:
                            pv = silu_done.get((e, ib - 1, tq))
                        if pv:
                            w(te, "s_ve" if ib == 0 else "s_sc", pv)
                    mm = te.matmul(pt[:, :],
                                   wg_sb[:, D * i + 128 * ib: D * i + 128 * (ib + 1)],
                                   x2t[:, 2048 * i + 512 * tq: 2048 * i + 512 * (tq + 1)],
                                   start=(i == 0), stop=(i == 7))
            gmm = inc(mm, "s_mm")
            w(te, "s_w", wu_v)
            for i in range(8):
                for tq in range(4):
                    pt = pA if tq < 2 else pB
                    sl = pt[:, 512 * (tq % 2):512 * (tq % 2 + 1)]
                    if i == 0:
                        pv = h_done.get((e - 1, 7, tq) if ib == 0 else (e, ib - 1, tq))
                        if pv:
                            w(te, "s_ve", pv)
                    mm = te.matmul(sl,
                                   wu_sb[:, D * i + 128 * ib: D * i + 128 * (ib + 1)],
                                   x2t[:, 2048 * i + 512 * tq: 2048 * i + 512 * (tq + 1)],
                                   start=(i == 0), stop=(i == 7))
            umm = inc(mm, "s_mm")
            for tq in range(4):
                w(sc, "s_mm", gmm)
                hv = h_done.get((e - 1, 7, tq) if ib == 0 else (e, ib - 1, tq))
                if hv:
                    w(sc, "s_ve", hv)
                silu_done[(e, ib, tq)] = inc(
                    sc.activation(silug[:, 512 * tq:512 * (tq + 1)],
                                  P512[tq][:, :], AF.Silu), "s_sc")
                w(ve, "s_mm", umm)
                w(ve, "s_sc", silu_done[(e, ib, tq)])
                pt = pA if tq < 2 else pB
                sl = pt[:, 512 * (tq % 2):512 * (tq % 2 + 1)]
                h_done[(e, ib, tq)] = inc(
                    ve.tensor_tensor(hbuf[:, 2048 * ib + 512 * tq: 2048 * ib + 512 * (tq + 1)],
                                     silug[:, 512 * tq:512 * (tq + 1)], sl, ALU.mult), "s_ve")
        uf_cnt[un_g] = CNT["s_mm"]
        uf_cnt[un_u] = CNT["s_mm"]
        w(te, "s_w", wd_v)
        dn_ev = {}
        for tb in range(16):
            for of in range(2):
                pt = P512[(2 * (tb % 2)) + of]
                pv = dn_ev.get((tb - 2, of))
                if pv:
                    w(te, "s_ve", pv)
                elif tb < 2 and e > 0:
                    pv2 = plan.get(("dnlast", e - 1, tb, of))
                    if pv2:
                        w(te, "s_ve", pv2)
                if tb == 0 and of == 0:
                    w(te, "s_ve", h_done[(e, 7, 3)])
                for gb in range(8):
                    mm = te.matmul(pt[:, :],
                                   hbuf[:, 2048 * gb + 128 * tb: 2048 * gb + 128 * (tb + 1)],
                                   wd_sb[:, D * gb + 512 * of: D * gb + 512 * (of + 1)],
                                   start=(gb == 0), stop=(gb == 7))
                mmv = inc(mm, "s_mm")
                w(ve, "s_mm", mmv)
                aslc = acc[:, 1024 * tb + 512 * of: 1024 * tb + 512 * (of + 1)]
                dn_ev[(tb, of)] = inc(
                    ve.scalar_tensor_tensor(aslc, pt[:, :],
                                            g_sb[:, NEX * tb + e:NEX * tb + e + 1],
                                            aslc, ALU.mult, ALU.add), "s_ve")
                plan[("dnlast", e, tb, of)] = dn_ev[(tb, of)]
                last_acc[tb] = dn_ev[(tb, of)]
        uf_cnt[un_d] = CNT["s_mm"]
        if e == NEX - 1:
            for tb in range(16):
                w(sy, "s_ve", last_acc[tb])
                out_cnt = dma(sy, y[128 * tb:128 * (tb + 1), :],
                              acc[:, 1024 * tb:1024 * (tb + 1)], "s_out")
        if e + 1 < NEX:
            for m, wsrc in ((0, wg_all), (1, wu_all), (2, wd_all)):
                un = (3 * (e + 1) + m) % 4
                w(gp, "s_mm", uf_cnt[un])
                plan[("wx", e + 1, m)] = dma(
                    gp, U[un][:].rearrange("p (i c) -> p i c", c=D),
                    wsrc[e + 1].rearrange("(i p) c -> p i c", p=128), "s_w")

    sy.wait_ge(SEM["s_out"], out_cnt)
    return nc


def _perm_rows(core):
    b, par = core // 2, core % 2
    chunks = CH_A if par == 0 else CH_B
    return np.concatenate([b * S + CHK * g + np.arange(CHK) for g in chunks])


def _mk_masks(core):
    par = core % 2
    chunks = CH_A if par == 0 else CH_B
    m = np.zeros((NSLOT, 4, 128, CHK), np.float32)
    p = np.arange(128)[:, None]
    cc = np.arange(CHK)[None, :]
    for sl in range(NSLOT):
        g = chunks[sl]
        for l in range(4):
            kt = SLOT_KT[sl] - 4 + l
            m[sl, l] = (128 * kt + p <= CHK * g + cc)
    return m.reshape(NSLOT * 4 * 128, CHK).astype(ml_dtypes.bfloat16)


def kernel(L0_attn_hidden_states, L0_attn_c_attn_kernel, L0_attn_c_attn_bias,
           L0_attn_c_proj_kernel, L0_attn_c_proj_bias, L0_ffn_gate_w,
           L0_ffn_expert_bias, L0_ffn_w1, L0_ffn_w2, L0_ffn_w3,
           L0_ffn_sw1, L0_ffn_sw2, L0_ffn_sw3):
    x = np.ascontiguousarray(np.asarray(L0_attn_hidden_states, np.float32)).reshape(B * S, D)
    wqkvT = np.ascontiguousarray(np.asarray(L0_attn_c_attn_kernel, np.float32).T)
    qkvb = np.ascontiguousarray(np.asarray(L0_attn_c_attn_bias, np.float32)
                                .reshape(24, 128).T)
    shb = np.ascontiguousarray(np.concatenate([
        np.broadcast_to(np.asarray(L0_attn_c_attn_bias, np.float32)[2048:3072], (128, D)),
        np.broadcast_to(np.asarray(L0_attn_c_proj_bias, np.float32), (128, D))], axis=0))
    wprojT = np.ascontiguousarray(np.asarray(L0_attn_c_proj_kernel, np.float32).T)
    gate_wT = np.ascontiguousarray(np.asarray(L0_ffn_gate_w, np.float32).T)
    egate = np.ascontiguousarray(np.broadcast_to(
        np.asarray(L0_ffn_expert_bias, np.float32), (128, NE)))
    w1 = np.asarray(L0_ffn_w1, np.float32)
    w2 = np.asarray(L0_ffn_w2, np.float32)
    w3 = np.asarray(L0_ffn_w3, np.float32)
    wg_all = np.ascontiguousarray(np.concatenate(
        [np.swapaxes(w1, 1, 2), np.asarray(L0_ffn_sw1, np.float32).T[None]], axis=0))
    wu_all = np.ascontiguousarray(np.concatenate(
        [np.swapaxes(w3, 1, 2), np.asarray(L0_ffn_sw3, np.float32).T[None]], axis=0))
    wd_all = np.ascontiguousarray(np.concatenate(
        [np.swapaxes(w2, 1, 2), np.asarray(L0_ffn_sw2, np.float32).T[None]], axis=0))

    if "nc" not in _CACHE:
        _CACHE["nc"] = build_nc()
    nc = _CACHE["nc"]

    in_maps = []
    perms = []
    for c in range(NCORES):
        rows = _perm_rows(c)
        perms.append(rows)
        bidx = c // 2
        in_maps.append({
            "x_q": np.ascontiguousarray(x[rows]),
            "x_f": np.ascontiguousarray(x[bidx * S:(bidx + 1) * S]),
            "masks": _mk_masks(c),
            "wqkvT": wqkvT, "qkv_bias": qkvb, "sh_bias": shb,
            "wprojT": wprojT, "gate_wT": gate_wT, "egate_b": egate,
            "wg_all": wg_all, "wu_all": wu_all, "wd_all": wd_all,
        })
    res = run_bass_kernel_spmd(nc, in_maps, core_ids=list(range(NCORES)))
    out = np.empty((B * S, D), np.float32)
    for cix in range(NCORES):
        out[perms[cix]] = np.asarray(res.results[cix]["y"], np.float32)
    if DBG.get("keep"):
        DBG["res"] = res
        DBG["perms"] = perms
    return out.reshape(B, S, D)


# version 05
# speedup vs baseline: 1.1059x; 1.1059x over previous
import numpy as np
import ml_dtypes
import concourse.bass as bass
import concourse.mybir as mybir
from concourse.bass_utils import run_bass_kernel_spmd

F32 = mybir.dt.float32
BF16 = mybir.dt.bfloat16
AF = mybir.ActivationFunctionType
ALU = mybir.AluOpType
AX = mybir.AxisListType

NCORES = 8
B, S, D = 4, 4096, 1024
NH, HDIM = 16, 64
NE, NEX, TOPK = 16, 17, 4
TOK, NSLOT, CHK = 2048, 8, 256
CH_A = [0, 15, 2, 13, 4, 11, 6, 9]
CH_B = [1, 14, 3, 12, 5, 10, 7, 8]
SLOT_KT = [max(2 * (a + 1), 2 * (b + 1)) for a, b in zip(CH_A, CH_B)]
LN_EPS = 1e-5
SM_SCALE = 0.125

DBG = {}
_CACHE = {}


def build_nc():
    nc = bass.Bass("TRN2", target_bir_lowering=False)

    x_q = nc.dram_tensor("x_q", [TOK, D], F32, kind="ExternalInput")
    x_f = nc.dram_tensor("x_f", [S, D], F32, kind="ExternalInput")
    masks = nc.dram_tensor("masks", [NSLOT * 4 * 128, CHK], BF16, kind="ExternalInput")
    wqkvT = nc.dram_tensor("wqkvT", [D, 3 * D], F32, kind="ExternalInput")
    qkv_bias = nc.dram_tensor("qkv_bias", [128, 24], F32, kind="ExternalInput")
    sh_bias = nc.dram_tensor("sh_bias", [2 * 128, D], F32, kind="ExternalInput")
    wprojT = nc.dram_tensor("wprojT", [D, D], F32, kind="ExternalInput")
    gate_wT = nc.dram_tensor("gate_wT", [D, NE], F32, kind="ExternalInput")
    egate_b = nc.dram_tensor("egate_b", [128, NE], F32, kind="ExternalInput")
    wg_all = nc.dram_tensor("wg_all", [NEX, D, D], F32, kind="ExternalInput")
    wu_all = nc.dram_tensor("wu_all", [NEX, D, D], F32, kind="ExternalInput")
    wd_all = nc.dram_tensor("wd_all", [NEX, D, D], F32, kind="ExternalInput")
    y = nc.dram_tensor("y", [TOK, D], F32, kind="ExternalOutput")

    spk = "ExternalOutput" if DBG.get("spills") else "Internal"
    kspill = nc.dram_tensor("kspill", [8, 128, S], BF16, kind=spk)
    vspill = nc.dram_tensor("vspill", [NH, 32, 128, 65], BF16, kind=spk)
    aotspill = nc.dram_tensor("aotspill", [8, 128, TOK], BF16, kind=spk)
    dbg_t = {}
    for name, shape, dt_ in (("qt_dbg", [128, 16384], BF16),
                             ("xqt_dbg", [128, 16384], BF16),
                             ("xh_dbg", [128, 16 * D], BF16),
                             ("out1_dbg", [128, 16 * D], F32),
                             ("x2t_dbg", [128, 8 * TOK], BF16),
                             ("g_dbg", [128, 16 * NEX], F32)):
        if DBG.get(name):
            dbg_t[name] = nc.dram_tensor(name, shape, dt_, kind="ExternalOutput")

    BASE = 16512
    KB = 1024

    def A(name, shape, dt_, off_b):
        assert (BASE + off_b) % 32 == 0, name
        return nc.alloc_sbuf_tensor_at(name, shape, dt_, offset=BASE + off_b)

    xqt = A("xqt", [128, 8 * TOK], BF16, 0)
    xst1 = A("xst1", [128, 2 * D], BF16, 32 * KB)
    xft = A("xft", [128, 8 * S], BF16, 0)
    vbuf = A("vbuf", [128, 2 * 4160], BF16, 0)
    acc = A("acc", [128, 16 * D], F32, 0)
    qt = A("qt", [128, 8 * TOK], BF16, 64 * KB)
    x2st = A("x2st", [128, 2 * D], BF16, 64 * KB)
    pstage = A("pstage", [128, 2 * D], BF16, 68 * KB)
    rtk = A("rtk", [128, 6 * 256], F32, 72 * KB)
    hbuf = A("hbuf", [128, 8 * TOK], BF16, 64 * KB)
    x2t = A("x2t", [128, 8 * TOK], BF16, 96 * KB)
    u0 = A("u0", [128, 8 * D], BF16, 128 * KB)
    u1 = A("u1", [128, 8 * D], BF16, 144 * KB)
    u2 = A("u2", [128, 8 * D], BF16, 160 * KB)
    u3 = A("u3", [128, 8 * D], BF16, 176 * KB)
    U = [u0, u1, u2, u3]
    wqkv_sb = u0
    kevac = A("kevac", [128, 2 * S], BF16, 144 * KB)
    masks_sb = u2
    vstage = A("vstage", [128, 2 * 1040], BF16, 176 * KB)
    xst2 = A("xst2", [128, 2 * D], BF16, 176 * KB + 4160)
    kbuf = A("kbuf", [128, 2 * S], BF16, 176 * KB)
    wproj_sb = u3
    xst = A("xst", [128, 2 * D], F32, 192 * KB)
    exps_a = A("exps_a", [128, 4 * CHK], BF16, 192 * KB)
    exps_b = A("exps_b", [128, 4 * CHK], BF16, 194 * KB)
    tmpo = A("tmpo", [128, CHK], F32, 196 * KB)
    tmpf = A("tmpf", [128, 2 * CHK], BF16, 197 * KB)
    rz_sb = A("rz_sb", [1, 2 * CHK], F32, 198 * KB)
    lnst = A("lnst", [128, 6 * 32], F32, int(184.5 * KB))
    lnst4 = A("lnst4", [128, 6 * 32], F32, 78 * KB)
    silug = A("silug", [128, 2048], BF16, 192 * KB)
    mis = [200 * KB]

    def M(name, shape, dt_):
        t = A(name, shape, dt_, mis[0])
        nby = int(np.prod(shape[1:])) * (2 if dt_ == BF16 else 4)
        mis[0] += (nby + 31) // 32 * 32
        return t

    shb_sb = M("shb_sb", [128, D], F32)
    qkvb_sb = M("qkvb_sb", [128, 24], F32)
    egate_sb = M("egate_sb", [128, NE], F32)
    gatew_sb = M("gatew_sb", [128, 8 * NE], BF16)
    g_sb = M("g_sb", [128, 16 * NEX], F32)
    lnc = M("lnc", [128, 8], F32)
    tk = M("tk", [128, 4 * NE], F32)
    ones64 = M("ones64", [1, 64], F32)
    bigp = M("bigp", [128, NE], F32)
    mcol = M("mcol", [128, 4], F32)
    assert mis[0] <= int(207.8 * KB), mis[0]

    pA = nc.psum_tensor("pA", [128, 1024], F32).__enter__()
    pB = nc.psum_tensor("pB", [128, 1024], F32).__enter__()
    p4 = nc.psum_tensor("p4", [128, 512], F32).__enter__()
    p5 = nc.psum_tensor("p5", [128, 512], F32).__enter__()
    p6 = nc.psum_tensor("p6", [128, 512], F32).__enter__()
    p7 = nc.psum_tensor("p7", [128, 512], F32).__enter__()
    P512 = [p4, p5, p6, p7]

    names = ["s_x", "s_w", "s_tr", "s_sp", "s_kv", "s_mm", "s_sc", "s_ve",
             "s_out", "s_uf0", "s_uf1", "s_uf2", "s_uf3"]
    SEM = {n: nc.semaphore(n).__enter__() for n in names}
    CNT = {n: 0 for n in names}

    te, sc, ve, sy, gp = nc.tensor, nc.scalar, nc.vector, nc.sync, nc.gpsimd

    def dma(eng, dst, src, sem):
        eng.dma_start(dst, src).then_inc(SEM[sem], 16)
        CNT[sem] += 16
        return CNT[sem]

    def dmat(eng, dst, src):
        eng.dma_start_transpose(dst, src).then_inc(SEM["s_tr"], 16)
        CNT["s_tr"] += 16
        return CNT["s_tr"]

    def inc(inst_or_eng, sem, via_sem_inc=False):
        if via_sem_inc:
            inst_or_eng.sem_inc(SEM[sem], 1)
        else:
            inst_or_eng.then_inc(SEM[sem], 1)
        CNT[sem] += 1
        return CNT[sem]

    def w(eng, sem, val):
        if val > 0:
            eng.wait_ge(SEM[sem], val)

    plan = {}

    p0_done = dma(sy, qkvb_sb[:], qkv_bias[:, :], "s_x")
    dma(gp, wqkv_sb[:].rearrange("p (i c) -> p i c", c=D),
        wqkvT[:, 0:D].rearrange("(i p) c -> p i c", p=128), "s_w")
    plan["wq0"] = CNT["s_w"]
    dma(gp, gatew_sb[:].rearrange("p (i c) -> p i c", c=NE),
        gate_wT.rearrange("(i p) c -> p i c", p=128), "s_w")
    plan["wgate"] = CNT["s_w"]
    plan["wq"] = CNT["s_w"]
    ve.memset(ones64[:], 1.0)
    ve.memset(bigp[:], -1e30)
    ve.memset(mcol[:, 3:4], LN_EPS)
    ve.memset(g_sb[:].rearrange("p (t e) -> p t e", e=NEX)[:, :, NE:NEX], 1.0)
    inc(ve.memset(vstage[:].rearrange("p (b h c) -> p b h c", h=NH, c=65)[:, :, :, 64:65], 1.0),
        "s_ve")
    if DBG.get("phase") == 0:
        w(sy, "s_x", CNT["s_x"])
        w(sy, "s_w", CNT["s_w"])
        w(sy, "s_ve", CNT["s_ve"])
        return nc

    ln_state = {"prev": 0}

    def ln_stats(xin, st, col, xv):
        w(sc, "s_ve", ln_state["prev"])
        i1 = inc(sc.activation(pA[:, 0:1024], xin, AF.Square), "s_sc")
        w(ve, "s_x", xv)
        inc(ve.reduce_sum(st[:, col:col + 1], xin, axis=AX.X), "s_ve")
        w(ve, "s_sc", i1)
        ln_state["prev"] = inc(
            ve.reduce_sum(st[:, 32 + col:33 + col], pA[:, 0:1024], axis=AX.X), "s_ve")

    def ln_chain(st, n):
        s1, s2 = st[:, 0:n], st[:, 32:32 + n]
        c4, c5 = st[:, 64:64 + n], st[:, 96:96 + n]
        rstd, nb = st[:, 128:128 + n], st[:, 160:160 + n]
        ve.wait_ge(SEM["s_ve"], ln_state["prev"])
        i1 = inc(ve.tensor_scalar(c4, s2, 1.0 / D, LN_EPS, ALU.mult, ALU.add), "s_ve")
        w(sc, "s_ve", i1)
        i2 = inc(sc.activation(c5, s1, AF.Square), "s_sc")
        w(ve, "s_sc", i2)
        i3 = inc(ve.scalar_tensor_tensor(c5, c5, -1.0 / (D * D), c4,
                                         ALU.mult, ALU.add), "s_ve")
        ve.wait_ge(SEM["s_ve"], i3)
        i4 = inc(ve.reciprocal(c4, c5), "s_ve")
        w(sc, "s_ve", i4)
        i5 = inc(sc.activation(rstd, c4, AF.Sqrt), "s_sc")
        w(ve, "s_sc", i5)
        i6 = inc(ve.tensor_tensor(nb, s1, rstd, ALU.mult), "s_ve")
        ve.wait_ge(SEM["s_ve"], i6)
        return inc(ve.tensor_scalar(nb, nb, -1.0 / D, None, ALU.mult), "s_ve")

    def ln_norm(xin, xout, st, col, chain_v):
        w(sc, "s_ve", chain_v)
        return inc(sc.activation(xout, xin, AF.Identity,
                                 bias=st[:, 160 + col:161 + col],
                                 scale=st[:, 128 + col:129 + col]), "s_sc")

    def emit_ln(xin, xout, xv=0):
        w(sc, "s_ve", ln_state["prev"])
        i1 = inc(sc.activation(pA[:, 0:1024], xin, AF.Square), "s_sc")
        w(ve, "s_x", xv)
        i0 = inc(ve.reduce_sum(lnc[:, 0:1], xin, axis=AX.X), "s_ve")
        w(ve, "s_sc", i1)
        iprev = inc(ve.reduce_sum(lnc[:, 1:2], pA[:, 0:1024], axis=AX.X), "s_ve")
        ln_state["prev"] = iprev
        ve.wait_ge(SEM["s_ve"], iprev)
        i2v = inc(ve.tensor_scalar(lnc[:, 4:5], lnc[:, 1:2], 1.0 / D, LN_EPS,
                                   ALU.mult, ALU.add), "s_ve")
        w(sc, "s_ve", i2v)
        sc.mul(lnc[:, 2:3], lnc[:, 0:1], lnc[:, 0:1])
        i2 = inc(sc.activation(lnc[:, 5:6], lnc[:, 2:3], AF.Identity,
                               bias=lnc[:, 4:5], scale=-1.0 / (D * D)), "s_sc")
        w(ve, "s_sc", i2)
        i3 = inc(ve.reciprocal(lnc[:, 6:7], lnc[:, 5:6]), "s_ve")
        w(sc, "s_ve", i3)
        i4 = inc(sc.activation(lnc[:, 7:8], lnc[:, 6:7], AF.Sqrt), "s_sc")
        w(ve, "s_sc", i4)
        i5 = inc(ve.tensor_scalar(lnc[:, 2:3], lnc[:, 7:8], lnc[:, 0:1],
                                  -1.0 / D, ALU.mult, ALU.mult), "s_ve")
        w(sc, "s_ve", i5)
        return inc(sc.activation(xout, xin, AF.Identity, bias=lnc[:, 2:3],
                                 scale=lnc[:, 7:8]), "s_sc")

    for tb in range(2):
        plan[("xq", tb)] = dma(sy, xst[:, (tb % 2) * D:(tb % 2 + 1) * D],
                               x_q[tb * 128:(tb + 1) * 128, :], "s_x")
    for tb in range(16):
        xin = xst[:, (tb % 2) * D:(tb % 2 + 1) * D]
        w(sc, "s_x", plan[("xq", tb)])
        ln_stats(xin, lnst, tb, plan[("xq", tb)])
        if tb + 2 < 16:
            w(sy, "s_ve", ln_state["prev"])
            plan[("xq", tb + 2)] = dma(sy, xst[:, (tb % 2) * D:(tb % 2 + 1) * D],
                                       x_q[(tb + 2) * 128:(tb + 3) * 128, :], "s_x")
    chv = ln_chain(lnst, 16)
    dma(sy, masks_sb[:].rearrange("p (s c) -> p s c", c=CHK),
        masks.rearrange("(s p) c -> p s c", p=128), "s_x")
    dma(sy, egate_sb[:], egate_b[:, :], "s_x")
    p0_done = dma(sy, shb_sb[:], sh_bias[0:128, :], "s_x")
    w(sy, "s_ve", ln_state["prev"])
    for tb in range(2):
        plan[("xqc", tb)] = dma(sy, xst[:, (tb % 2) * D:(tb % 2 + 1) * D],
                                x_q[tb * 128:(tb + 1) * 128, :], "s_x")
    tr_q = []
    for tb in range(16):
        xin = xst[:, (tb % 2) * D:(tb % 2 + 1) * D]
        xout = xst1[:, (tb % 2) * D:(tb % 2 + 1) * D]
        w(sc, "s_x", plan[("xqc", tb)])
        if tb >= 2:
            w(sc, "s_tr", tr_q[tb - 2])
        nrm = ln_norm(xin, xout, lnst, tb, chv)
        w(sc, "s_sc", nrm)
        w(sy, "s_sc", nrm)
        for i in range(8):
            dst = xqt[:, 2048 * i + 128 * tb: 2048 * i + 128 * (tb + 1)]
            src = xout[:, 128 * i:128 * (i + 1)]
            dmat(sc if i % 2 == 0 else sy, dst, src)
        tr_q.append(CNT["s_tr"])
        if tb + 2 < 16:
            w(sy, "s_sc", nrm)
            plan[("xqc", tb + 2)] = dma(sy, xst[:, (tb % 2) * D:(tb % 2 + 1) * D],
                                        x_q[(tb + 2) * 128:(tb + 3) * 128, :], "s_x")
    w(te, "s_w", plan["wq0"])
    w(sc, "s_x", p0_done)
    qev = {}
    for tt in range(4):
        w(te, "s_tr", tr_q[4 * tt + 3])
        for ob in range(8):
            pt = P512[ob % 4]
            key = (tt, ob - 4) if ob >= 4 else (tt - 1, ob + 4)
            if key in qev:
                w(te, "s_sc", qev[key])
            for i in range(8):
                mm = te.matmul(pt[:, :],
                               wqkv_sb[:, D * i + 128 * ob: D * i + 128 * (ob + 1)],
                               xqt[:, 2048 * i + 512 * tt: 2048 * i + 512 * (tt + 1)],
                               start=(i == 0), stop=(i == 7))
            mmv = inc(mm, "s_mm")
            w(sc, "s_mm", mmv)
            qev[(tt, ob)] = inc(
                sc.activation(qt[:, 2048 * ob + 512 * tt: 2048 * ob + 512 * (tt + 1)],
                              pt[:, :], AF.Identity, bias=qkvb_sb[:, ob:ob + 1]), "s_sc")
    plan["qdone_mm"] = CNT["s_mm"]
    plan["qdone_ev"] = CNT["s_sc"]
    if DBG.get("qt_dbg"):
        w(sy, "s_sc", plan["qdone_ev"])
        dma(sy, dbg_t["qt_dbg"][:, :], qt[:], "s_out")
    if DBG.get("xqt_dbg"):
        w(sy, "s_tr", tr_q[15])
        dma(sy, dbg_t["xqt_dbg"][:, :], xqt[:], "s_out")
    if DBG.get("xh_dbg"):
        w(sy, "s_ve", CNT["s_ve"])
        dma(sy, dbg_t["xh_dbg"][:, :], x2t[:, 0:16 * D], "s_out")
    if DBG.get("phase") == 1:
        sy.wait_ge(SEM["s_out"], CNT["s_out"])
        return nc

    w(gp, "s_mm", plan["qdone_mm"])
    plan["wk"] = dma(gp, wqkv_sb[:].rearrange("p (i c) -> p i c", c=D),
                     wqkvT[:, D:2 * D].rearrange("(i p) c -> p i c", p=128), "s_w")
    for tb in range(2):
        if tb == 0:
            w(sy, "s_sc", plan["qdone_ev"])
        plan[("xf", tb)] = dma(sy, xst[:, (tb % 2) * D:(tb % 2 + 1) * D],
                               x_f[tb * 128:(tb + 1) * 128, :], "s_x")
    for tb in range(32):
        xin = xst[:, (tb % 2) * D:(tb % 2 + 1) * D]
        w(sc, "s_x", plan[("xf", tb)])
        ln_stats(xin, lnst, tb, plan[("xf", tb)])
        if tb + 2 < 32:
            w(sy, "s_ve", ln_state["prev"])
            plan[("xf", tb + 2)] = dma(sy, xst[:, (tb % 2) * D:(tb % 2 + 1) * D],
                                       x_f[(tb + 2) * 128:(tb + 3) * 128, :], "s_x")
    chv = ln_chain(lnst, 32)
    w(sy, "s_ve", ln_state["prev"])
    for tb in range(2):
        plan[("xfc", tb)] = dma(sy, xst[:, (tb % 2) * D:(tb % 2 + 1) * D],
                                x_f[tb * 128:(tb + 1) * 128, :], "s_x")
    tr_f = []
    for tb in range(32):
        xin = xst[:, (tb % 2) * D:(tb % 2 + 1) * D]
        xout = xst2[:, (tb % 2) * D:(tb % 2 + 1) * D]
        w(sc, "s_x", plan[("xfc", tb)])
        if tb == 0:
            w(sc, "s_mm", plan["qdone_mm"])
        if tb >= 2:
            w(sc, "s_tr", tr_f[tb - 2])
        nrm = ln_norm(xin, xout, lnst, tb, chv)
        w(sc, "s_sc", nrm)
        w(sy, "s_sc", nrm)
        for i in range(8):
            dst = xft[:, 4096 * i + 128 * tb: 4096 * i + 128 * (tb + 1)]
            src = xout[:, 128 * i:128 * (i + 1)]
            dmat(sc if i % 2 == 0 else sy, dst, src)
        tr_f.append(CNT["s_tr"])
        if tb + 2 < 32:
            w(sy, "s_sc", nrm)
            plan[("xfc", tb + 2)] = dma(sy, xst[:, (tb % 2) * D:(tb % 2 + 1) * D],
                                        x_f[(tb + 2) * 128:(tb + 3) * 128, :], "s_x")
    w(te, "s_w", plan["wk"])
    kev = {}
    ksp = {}
    for ob in range(8):
        for tt in range(8):
            w(te, "s_tr", tr_f[4 * tt + 3])
            idx = 8 * ob + tt
            pt = P512[idx % 4]
            if idx >= 4:
                pk = (idx - 4) // 8, (idx - 4) % 8
                w(te, "s_sc", kev[pk])
            for i in range(8):
                mm = te.matmul(pt[:, :],
                               wqkv_sb[:, D * i + 128 * ob: D * i + 128 * (ob + 1)],
                               xft[:, 4096 * i + 512 * tt: 4096 * i + 512 * (tt + 1)],
                               start=(i == 0), stop=(i == 7))
            mmv = inc(mm, "s_mm")
            w(sc, "s_mm", mmv)
            if tt == 0 and ob >= 2:
                w(sc, "s_sp", ksp[ob - 2])
            kev[(ob, tt)] = inc(
                sc.activation(kevac[:, (ob % 2) * S + 512 * tt:(ob % 2) * S + 512 * (tt + 1)],
                              pt[:, :], AF.Identity, bias=qkvb_sb[:, 8 + ob:9 + ob]), "s_sc")
        w(sy, "s_sc", kev[(ob, 7)])
        ksp[ob] = dma(sy, kspill[ob, :, :], kevac[:, (ob % 2) * S:(ob % 2 + 1) * S], "s_sp")
    plan["kdone_mm"] = CNT["s_mm"]
    w(gp, "s_mm", plan["kdone_mm"])
    plan["wv"] = dma(gp, wqkv_sb[:].rearrange("p (i c) -> p i c", c=D),
                     wqkvT[:, 2 * D:3 * D].rearrange("(i p) c -> p i c", p=128), "s_w")
    w(te, "s_w", plan["wv"])
    w(ve, "s_x", p0_done)
    vev = {}
    vsp = {}
    for jj in range(32):
        for vfb in range(2):
            idx = 2 * jj + vfb
            pt = P512[idx % 4]
            if idx >= 4:
                w(te, "s_ve", vev[((idx - 4) // 2, (idx - 4) % 2)])
            for i in range(8):
                mm = te.matmul(pt[:, :],
                               xft[:, 4096 * i + 128 * jj: 4096 * i + 128 * (jj + 1)],
                               wqkv_sb[:, D * i + 512 * vfb: D * i + 512 * (vfb + 1)],
                               start=(i == 0), stop=(i == 7))
            mmv = inc(mm, "s_mm")
            w(ve, "s_mm", mmv)
            if vfb == 0 and jj >= 2:
                w(ve, "s_sp", vsp[jj - 2])
            dst = vstage[:, (jj % 2) * 1040 + 520 * vfb:(jj % 2) * 1040 + 520 * (vfb + 1)]
            vev[(jj, vfb)] = inc(
                ve.tensor_tensor(dst.rearrange("p (h c) -> p h c", c=65)[:, :, 0:64],
                                 pt[:].rearrange("p (h c) -> p h c", c=64),
                                 shb_sb[:, 512 * vfb:512 * (vfb + 1)]
                                 .rearrange("p (h c) -> p h c", c=64), ALU.add), "s_ve")
        w(sy, "s_ve", vev[(jj, 1)])
        vsp[jj] = dma(sy, vspill[:, jj, :, :].rearrange("h p c -> p h c"),
                      vstage[:, (jj % 2) * 1040:(jj % 2 + 1) * 1040]
                      .rearrange("p (h c) -> p h c", c=65), "s_sp")
    plan["vdone_mm"] = CNT["s_mm"]
    plan["vdone_sp"] = CNT["s_sp"]

    w(gp, "s_mm", plan["vdone_mm"])
    w(gp, "s_sp", ksp[7])
    plan[("wg", 0)] = dma(gp, u0[:].rearrange("p (i c) -> p i c", c=D),
                          wg_all[0].rearrange("(i p) c -> p i c", p=128), "s_w")
    plan[("wu", 0)] = dma(gp, u1[:].rearrange("p (i c) -> p i c", c=D),
                          wu_all[0].rearrange("(i p) c -> p i c", p=128), "s_w")
    if DBG.get("phase") == 2:
        w(sy, "s_sp", CNT["s_sp"])
        sy.wait_ge(SEM["s_out"], CNT["s_out"])
        return nc

    kv_ready = {}

    def load_pair(j):
        if j < 2:
            w(sy, "s_sp", plan["vdone_sp"])
            w(sy, "s_tr", tr_f[31])
        else:
            w(sy, "s_mm", plan[("pair_mm_done", j - 2)])
        w(sy, "s_sp", ksp[j])
        dma(sy, kbuf[:, (j % 2) * S:(j % 2 + 1) * S], kspill[j, :, :], "s_kv")
        if j < 2:
            w(sy, "s_mm", plan["vdone_mm"])
        dma(sy, vbuf[:, (j % 2) * 4160:(j % 2) * 4160 + 2080]
            .rearrange("p (jj c) -> p jj c", c=65),
            vspill[2 * j].rearrange("jj p c -> p jj c"), "s_kv")
        kv_ready[j] = dma(sy, vbuf[:, (j % 2) * 4160 + 2080:(j % 2) * 4160 + 4160]
                          .rearrange("p (jj c) -> p jj c", c=65),
                          vspill[2 * j + 1].rearrange("jj p c -> p jj c"), "s_kv")

    w(ve, "s_x", p0_done)
    load_pair(0)
    load_pair(1)
    fin_prev = {}
    av_bank_free = {0: 0, 1: 0}
    zb_free = 0
    exp_done = {}
    glob_of = {}
    slot_of = {}
    gctr = [0]
    aotsp = []
    for j in range(8):
        w(te, "s_ve", CNT["s_ve"])
        w(te, "s_sc", CNT["s_sc"])
        kof = (j % 2) * S
        vof = (j % 2) * 4160
        w(te, "s_kv", kv_ready[j])
        for s in range(NSLOT):
            skt = SLOT_KT[s]
            ng = skt // 4
            for g in range(ng):
                smm = {}
                for par, kb0 in ((0, 0), (1, 64)):
                    glob = gctr[0]; gctr[0] += 1
                    sl = glob % 2
                    pS = pA if sl == 0 else pB
                    slot_of[(par, s, g)] = sl
                    pv = exp_done.get(glob - 2)
                    if pv:
                        w(te, "s_sc", pv)
                    for q in range(4):
                        kt = 4 * g + q
                        mm = te.matmul(pS[:, 256 * q:256 * (q + 1)],
                                       kbuf[kb0:kb0 + 64, kof + 128 * kt: kof + 128 * (kt + 1)],
                                       qt[kb0:kb0 + 64, 2048 * j + 256 * s: 2048 * j + 256 * (s + 1)],
                                       start=True, stop=True)
                    smm[par] = inc(mm, "s_mm")
                    glob_of[(par, s, g)] = glob
                for par in range(2):
                    glob = glob_of[(par, s, g)]
                    sl = slot_of[(par, s, g)]
                    pS = pA if sl == 0 else pB
                    ex = exps_a if sl == 0 else exps_b
                    w(sc, "s_mm", smm[1])
                    pv = plan.get(("avg", glob - 2))
                    if pv:
                        w(sc, "s_mm", pv)
                    exv = inc(sc.activation(ex[:], pS[:, 0:1024], AF.Exp,
                                            scale=SM_SCALE), "s_sc")
                    exp_done[glob] = exv
                    gate_sem, gate_val = "s_sc", exv
                    if g == ng - 1:
                        w(ve, "s_sc", exv)
                        mks = masks_sb[:, 1024 * s:1024 * (s + 1)]
                        gate_val = inc(ve.tensor_tensor(ex[:], ex[:], mks, ALU.mult), "s_ve")
                        gate_sem = "s_ve"
                    w(te, gate_sem, gate_val)
                    pO = p4 if par == 0 else p5
                    if g == 0 and av_bank_free[par]:
                        w(te, "s_ve", av_bank_free[par])
                    for q in range(4):
                        kt = 4 * g + q
                        mm = te.matmul(pO[0:65, 0:256],
                                       vbuf[:, vof + 2080 * par + 65 * kt: vof + 2080 * par + 65 * (kt + 1)],
                                       ex[:, 256 * q:256 * (q + 1)],
                                       start=(kt == 0), stop=(kt == skt - 1))
                    plan[("avg", glob)] = inc(mm, "s_mm")
                    plan[("avdone", par)] = plan[("avg", glob)]
            for par in range(2):
                pO = p4 if par == 0 else p5
                w(ve, "s_mm", plan[("avdone", par)])
                rz = rz_sb[0:1, 256 * par:256 * (par + 1)]
                cv = inc(ve.tensor_copy(tmpo[0:64, 0:256], pO[0:64, 0:256]), "s_ve")
                rv = inc(ve.reciprocal(rz, pO[64:65, 0:256]), "s_ve")
                av_bank_free[par] = rv
                w(te, "s_ve", rv)
                if zb_free:
                    w(te, "s_ve", zb_free)
                zbv = inc(te.matmul(p6[0:64, 0:256], ones64[0:1, 0:64], rz,
                                    start=True, stop=True), "s_mm")
                w(ve, "s_mm", zbv)
                if len(aotsp) >= 2:
                    w(ve, "s_sp", aotsp[-2])
                fv = inc(ve.tensor_tensor(tmpf[0:64, 256 * par:256 * (par + 1)],
                                          tmpo[0:64, 0:256], p6[0:64, 0:256],
                                          ALU.mult), "s_ve")
                zb_free = fv
                w(sy, "s_ve", fv)
                aotsp.append(dma(sy, aotspill[j, 64 * par:64 * (par + 1), 256 * s:256 * (s + 1)],
                                 tmpf[0:64, 256 * par:256 * (par + 1)], "s_sp"))
        plan[("pair_mm_done", j)] = CNT["s_mm"]
        if j + 2 < 8:
            load_pair(j + 2)
    plan["attn_mm_done"] = CNT["s_mm"]
    plan["attn_ve_done"] = CNT["s_ve"]
    plan["attn_sp_done"] = CNT["s_sp"]
    if DBG.get("phase") == 3:
        w(sy, "s_sp", CNT["s_sp"])
        sy.wait_ge(SEM["s_out"], CNT["s_out"])
        return nc

    w(gp, "s_mm", plan["attn_mm_done"])
    plan["wproj"] = dma(gp, wproj_sb[:].rearrange("p (i c) -> p i c", c=D),
                        wprojT.rearrange("(i p) c -> p i c", p=128), "s_w")
    w(sy, "s_ve", plan["attn_ve_done"])
    plan["pbias"] = dma(sy, shb_sb[:], sh_bias[128:256, :], "s_x")
    pst = {}

    def p4_loads(tb):
        if tb == 0:
            w(sy, "s_sp", plan["attn_sp_done"])
            w(sy, "s_ve", plan["attn_ve_done"])
            w(sy, "s_sc", CNT["s_sc"])
        if tb >= 2:
            w(sy, "s_mm", plan[("projmm", tb - 2)])
            w(sy, "s_ve", plan[("acc_ev", tb - 2)])
        pst[tb] = dma(sy, pstage[:, (tb % 2) * D:(tb % 2 + 1) * D]
                      .rearrange("p (ab c) -> p ab c", c=128),
                      aotspill[:, :, 128 * tb:128 * (tb + 1)]
                      .rearrange("ab p c -> p ab c"), "s_x")
        plan[("xq2", tb)] = dma(sy, xst[:, (tb % 2) * D:(tb % 2 + 1) * D],
                                x_q[tb * 128:(tb + 1) * 128, :], "s_x")

    p4_loads(0)
    p4_loads(1)
    w(te, "s_w", plan["wproj"])
    prev_ev = {}
    tr2 = []
    gdone = {}
    for tb in range(16):
        w(te, "s_x", pst[tb])
        for of in range(2):
            pt = p4 if of == 0 else p5
            if tb > 0:
                w(te, "s_ve", prev_ev[(tb - 1, of)])
            for ab in range(8):
                mm = te.matmul(pt[:, :],
                               pstage[:, (tb % 2) * D + 128 * ab:(tb % 2) * D + 128 * (ab + 1)],
                               wproj_sb[:, D * ab + 512 * of: D * ab + 512 * (of + 1)],
                               start=(ab == 0), stop=(ab == 7))
            mmv = inc(mm, "s_mm")
            plan[("projmm", tb)] = CNT["s_mm"]
            w(ve, "s_mm", mmv)
            w(ve, "s_x", max(plan[("xq2", tb)], plan["pbias"]))
            aslc = acc[:, 1024 * tb + 512 * of: 1024 * tb + 512 * (of + 1)]
            ve.tensor_tensor(aslc, pt[:, :], shb_sb[:, 512 * of:512 * (of + 1)], ALU.add)
            prev_ev[(tb, of)] = inc(
                ve.tensor_tensor(aslc, aslc,
                                 xst[:, (tb % 2) * D + 512 * of:(tb % 2) * D + 512 * (of + 1)],
                                 ALU.add), "s_ve")
            plan[("acc_ev", tb)] = prev_ev[(tb, of)]
            if of == 1 and tb + 2 < 16:
                p4_loads(tb + 2)
        w(sc, "s_ve", prev_ev[(tb, 1)])
        ln_stats(acc[:, 1024 * tb:1024 * (tb + 1)], lnst4, tb, 0)
    chv = ln_chain(lnst4, 16)
    for tb in range(16):
        if tb == 0:
            w(sc, "s_mm", plan["attn_mm_done"])
        if tb >= 2:
            w(sc, "s_tr", tr2[tb - 2])
        xout = x2st[:, (tb % 2) * D:(tb % 2 + 1) * D]
        nrm = ln_norm(acc[:, 1024 * tb:1024 * (tb + 1)], xout, lnst4, tb, chv)
        w(sc, "s_sc", nrm)
        w(sy, "s_sc", nrm)
        for i in range(8):
            dst = x2t[:, 2048 * i + 128 * tb: 2048 * i + 128 * (tb + 1)]
            src = xout[:, 128 * i:128 * (i + 1)]
            dmat(sc if i % 2 == 0 else sy, dst, src)
        tr2.append(CNT["s_tr"])
        w(te, "s_tr", tr2[tb])
        w(te, "s_w", plan["wgate"])
        for i in range(8):
            mm = te.matmul(p7[:, NE * tb:NE * (tb + 1)],
                           x2t[:, 2048 * i + 128 * tb: 2048 * i + 128 * (tb + 1)],
                           gatew_sb[:, NE * i:NE * (i + 1)],
                           start=(i == 0), stop=(i == 7))
        mmv = inc(mm, "s_mm")
        plan[("rtmm", tb)] = mmv
    def bc(ap16):
        return bass.AP(ap16.tensor, ap16.offset,
                       [ap16.ap[0], [0, 16], ap16.ap[1]])

    def bce(ap16):
        return bass.AP(ap16.tensor, ap16.offset,
                       [ap16.ap[0], ap16.ap[1], [0, 16]])

    def vbar(v_inst):
        val = inc(v_inst, "s_ve")
        ve.wait_ge(SEM["s_ve"], val)
        return val

    g3 = lambda t: t.rearrange("p (g c) -> p g c", c=NE)
    lg, bi = rtk[:, 0:256], rtk[:, 256:512]
    wka, wkb = rtk[:, 512:768], rtk[:, 768:1024]
    scr, u = rtk[:, 1024:1280], rtk[:, 1280:1536]
    mred = mcol_r = rtk[:, 1280:1296]
    w(ve, "s_mm", plan[("rtmm", 15)])
    vbar(ve.tensor_copy(lg, p7[:, 0:256]))
    vbar(ve.tensor_tensor(g3(bi), g3(lg), bc(egate_sb[:, 0:NE]), ALU.add))
    vbar(ve.tensor_copy(wka, bi))
    cur, nxt = wka, wkb
    for r in range(4):
        vbar(ve.reduce_max(tk[:, 0:NE].rearrange("p (g one) -> p g one", one=1),
                           g3(cur), axis=AX.X))
        if r < 3:
            vbar(ve.tensor_tensor(g3(scr), g3(cur), bce(tk[:, 0:NE]), ALU.is_lt))
            vbar(ve.tensor_tensor(nxt, cur, scr, ALU.mult))
            vbar(ve.scalar_tensor_tensor(g3(u), g3(scr), 1e30, bc(bigp[:, 0:NE]),
                                         ALU.mult, ALU.add))
            vbar(ve.tensor_tensor(nxt, nxt, u, ALU.add))
            cur, nxt = nxt, cur
    vbar(ve.tensor_tensor(g3(scr), g3(bi), bce(tk[:, 0:NE]), ALU.is_lt))
    sel = vbar(ve.tensor_scalar(scr, scr, -1.0, 1.0, ALU.mult, ALU.add))
    w(sc, "s_ve", sel)
    epv = inc(sc.activation(lg, lg, AF.Exp), "s_sc")
    w(ve, "s_sc", epv)
    vbar(ve.tensor_tensor(wka, lg, scr, ALU.mult))
    vbar(ve.reduce_sum(tk[:, NE:2 * NE].rearrange("p (g one) -> p g one", one=1),
                       g3(wka), axis=AX.X))
    vbar(ve.reciprocal(tk[:, 2 * NE:3 * NE], tk[:, NE:2 * NE]))
    gdone = vbar(ve.tensor_tensor(
        g_sb[:].rearrange("p (t e) -> p t e", e=NEX)[:, :, 0:NE],
        g3(wka), bce(tk[:, 2 * NE:3 * NE]), ALU.mult))
    plan["p4_mm_done"] = CNT["s_mm"]
    plan["p4_ve_done"] = CNT["s_ve"]
    plan["p4_tr_done"] = CNT["s_tr"]

    if DBG.get("out1_dbg"):
        w(sy, "s_ve", plan["p4_ve_done"])
        w(sy, "s_tr", plan["p4_tr_done"])
        dma(sy, dbg_t["out1_dbg"][:, :], acc[:], "s_out")
        dma(sy, dbg_t["x2t_dbg"][:, :], x2t[:], "s_out")
        dma(sy, dbg_t["g_dbg"][:, :], g_sb[:], "s_out")

    if DBG.get("phase") == 4:
        sy.wait_ge(SEM["s_out"], CNT["s_out"])
        return nc
    w(gp, "s_ve", plan["attn_ve_done"])
    plan[("wd", 0)] = dma(gp, u2[:].rearrange("p (i c) -> p i c", c=D),
                          wd_all[0].rearrange("(i p) c -> p i c", p=128), "s_w")

    uf_cnt = {0: 0, 1: 0, 2: 0, 3: plan["p4_mm_done"]}
    wkey = {0: (plan[("wg", 0)], plan[("wu", 0)], plan[("wd", 0)])}
    last_acc = {tb: plan["p4_ve_done"] for tb in range(16)}
    silu_done = {}
    h_done = {}
    out_cnt = 0
    for e in range(NEX):
        un_g, un_u, un_d = (3 * e) % 4, (3 * e + 1) % 4, (3 * e + 2) % 4
        if e > 0:
            wg_v = plan[("wx", e, 0)]
            wu_v = plan[("wx", e, 1)]
            wd_v = plan[("wx", e, 2)]
        else:
            wg_v, wu_v, wd_v = wkey[0]
        wg_sb, wu_sb, wd_sb = U[un_g], U[un_u], U[un_d]
        w(te, "s_w", wg_v)
        if e == 0:
            w(te, "s_tr", plan["p4_tr_done"])
            w(te, "s_ve", plan["p4_ve_done"])
        for ib in range(8):
            for i in range(8):
                for tq in range(4):
                    pt = P512[tq]
                    if i == 0:
                        if ib == 0:
                            pv = plan.get(("dnlast", e - 1, 14 + tq // 2, tq % 2))
                        else:
                            pv = silu_done.get((e, ib - 1, tq))
                        if pv:
                            w(te, "s_ve" if ib == 0 else "s_sc", pv)
                    mm = te.matmul(pt[:, :],
                                   wg_sb[:, D * i + 128 * ib: D * i + 128 * (ib + 1)],
                                   x2t[:, 2048 * i + 512 * tq: 2048 * i + 512 * (tq + 1)],
                                   start=(i == 0), stop=(i == 7))
            gmm = inc(mm, "s_mm")
            w(te, "s_w", wu_v)
            for i in range(8):
                for tq in range(4):
                    pt = pA if tq < 2 else pB
                    sl = pt[:, 512 * (tq % 2):512 * (tq % 2 + 1)]
                    if i == 0:
                        pv = h_done.get((e - 1, 7, tq) if ib == 0 else (e, ib - 1, tq))
                        if pv:
                            w(te, "s_ve", pv)
                    mm = te.matmul(sl,
                                   wu_sb[:, D * i + 128 * ib: D * i + 128 * (ib + 1)],
                                   x2t[:, 2048 * i + 512 * tq: 2048 * i + 512 * (tq + 1)],
                                   start=(i == 0), stop=(i == 7))
            umm = inc(mm, "s_mm")
            for tq in range(4):
                w(sc, "s_mm", gmm)
                hv = h_done.get((e - 1, 7, tq) if ib == 0 else (e, ib - 1, tq))
                if hv:
                    w(sc, "s_ve", hv)
                silu_done[(e, ib, tq)] = inc(
                    sc.activation(silug[:, 512 * tq:512 * (tq + 1)],
                                  P512[tq][:, :], AF.Silu), "s_sc")
                w(ve, "s_mm", umm)
                w(ve, "s_sc", silu_done[(e, ib, tq)])
                pt = pA if tq < 2 else pB
                sl = pt[:, 512 * (tq % 2):512 * (tq % 2 + 1)]
                h_done[(e, ib, tq)] = inc(
                    ve.tensor_tensor(hbuf[:, 2048 * ib + 512 * tq: 2048 * ib + 512 * (tq + 1)],
                                     silug[:, 512 * tq:512 * (tq + 1)], sl, ALU.mult), "s_ve")
        uf_cnt[un_g] = CNT["s_mm"]
        uf_cnt[un_u] = CNT["s_mm"]
        w(te, "s_w", wd_v)
        dn_ev = {}
        for tb in range(16):
            for of in range(2):
                pt = P512[(2 * (tb % 2)) + of]
                pv = dn_ev.get((tb - 2, of))
                if pv:
                    w(te, "s_ve", pv)
                elif tb < 2 and e > 0:
                    pv2 = plan.get(("dnlast", e - 1, tb, of))
                    if pv2:
                        w(te, "s_ve", pv2)
                if tb == 0 and of == 0:
                    w(te, "s_ve", h_done[(e, 7, 3)])
                for gb in range(8):
                    mm = te.matmul(pt[:, :],
                                   hbuf[:, 2048 * gb + 128 * tb: 2048 * gb + 128 * (tb + 1)],
                                   wd_sb[:, D * gb + 512 * of: D * gb + 512 * (of + 1)],
                                   start=(gb == 0), stop=(gb == 7))
                mmv = inc(mm, "s_mm")
                w(ve, "s_mm", mmv)
                aslc = acc[:, 1024 * tb + 512 * of: 1024 * tb + 512 * (of + 1)]
                dn_ev[(tb, of)] = inc(
                    ve.scalar_tensor_tensor(aslc, pt[:, :],
                                            g_sb[:, NEX * tb + e:NEX * tb + e + 1],
                                            aslc, ALU.mult, ALU.add), "s_ve")
                plan[("dnlast", e, tb, of)] = dn_ev[(tb, of)]
                last_acc[tb] = dn_ev[(tb, of)]
        uf_cnt[un_d] = CNT["s_mm"]
        if e == NEX - 1:
            for tb in range(16):
                w(sy, "s_ve", last_acc[tb])
                out_cnt = dma(sy, y[128 * tb:128 * (tb + 1), :],
                              acc[:, 1024 * tb:1024 * (tb + 1)], "s_out")
        if e + 1 < NEX:
            for m, wsrc in ((0, wg_all), (1, wu_all), (2, wd_all)):
                un = (3 * (e + 1) + m) % 4
                w(gp, "s_mm", uf_cnt[un])
                plan[("wx", e + 1, m)] = dma(
                    gp, U[un][:].rearrange("p (i c) -> p i c", c=D),
                    wsrc[e + 1].rearrange("(i p) c -> p i c", p=128), "s_w")

    sy.wait_ge(SEM["s_out"], out_cnt)
    return nc


def _perm_rows(core):
    b, par = core // 2, core % 2
    chunks = CH_A if par == 0 else CH_B
    return np.concatenate([b * S + CHK * g + np.arange(CHK) for g in chunks])


def _mk_masks(core):
    par = core % 2
    chunks = CH_A if par == 0 else CH_B
    m = np.zeros((NSLOT, 4, 128, CHK), np.float32)
    p = np.arange(128)[:, None]
    cc = np.arange(CHK)[None, :]
    for sl in range(NSLOT):
        g = chunks[sl]
        for l in range(4):
            kt = SLOT_KT[sl] - 4 + l
            m[sl, l] = (128 * kt + p <= CHK * g + cc)
    return m.reshape(NSLOT * 4 * 128, CHK).astype(ml_dtypes.bfloat16)


def kernel(L0_attn_hidden_states, L0_attn_c_attn_kernel, L0_attn_c_attn_bias,
           L0_attn_c_proj_kernel, L0_attn_c_proj_bias, L0_ffn_gate_w,
           L0_ffn_expert_bias, L0_ffn_w1, L0_ffn_w2, L0_ffn_w3,
           L0_ffn_sw1, L0_ffn_sw2, L0_ffn_sw3):
    x = np.ascontiguousarray(np.asarray(L0_attn_hidden_states, np.float32)).reshape(B * S, D)
    wqkvT = np.ascontiguousarray(np.asarray(L0_attn_c_attn_kernel, np.float32).T)
    qkvb = np.ascontiguousarray(np.asarray(L0_attn_c_attn_bias, np.float32)
                                .reshape(24, 128).T)
    shb = np.ascontiguousarray(np.concatenate([
        np.broadcast_to(np.asarray(L0_attn_c_attn_bias, np.float32)[2048:3072], (128, D)),
        np.broadcast_to(np.asarray(L0_attn_c_proj_bias, np.float32), (128, D))], axis=0))
    wprojT = np.ascontiguousarray(np.asarray(L0_attn_c_proj_kernel, np.float32).T)
    gate_wT = np.ascontiguousarray(np.asarray(L0_ffn_gate_w, np.float32).T)
    egate = np.ascontiguousarray(np.broadcast_to(
        np.asarray(L0_ffn_expert_bias, np.float32), (128, NE)))
    w1 = np.asarray(L0_ffn_w1, np.float32)
    w2 = np.asarray(L0_ffn_w2, np.float32)
    w3 = np.asarray(L0_ffn_w3, np.float32)
    wg_all = np.ascontiguousarray(np.concatenate(
        [np.swapaxes(w1, 1, 2), np.asarray(L0_ffn_sw1, np.float32).T[None]], axis=0))
    wu_all = np.ascontiguousarray(np.concatenate(
        [np.swapaxes(w3, 1, 2), np.asarray(L0_ffn_sw3, np.float32).T[None]], axis=0))
    wd_all = np.ascontiguousarray(np.concatenate(
        [np.swapaxes(w2, 1, 2), np.asarray(L0_ffn_sw2, np.float32).T[None]], axis=0))

    if "nc" not in _CACHE:
        _CACHE["nc"] = build_nc()
    nc = _CACHE["nc"]

    in_maps = []
    perms = []
    for c in range(NCORES):
        rows = _perm_rows(c)
        perms.append(rows)
        bidx = c // 2
        in_maps.append({
            "x_q": np.ascontiguousarray(x[rows]),
            "x_f": np.ascontiguousarray(x[bidx * S:(bidx + 1) * S]),
            "masks": _mk_masks(c),
            "wqkvT": wqkvT, "qkv_bias": qkvb, "sh_bias": shb,
            "wprojT": wprojT, "gate_wT": gate_wT, "egate_b": egate,
            "wg_all": wg_all, "wu_all": wu_all, "wd_all": wd_all,
        })
    res = run_bass_kernel_spmd(nc, in_maps, core_ids=list(range(NCORES)))
    out = np.empty((B * S, D), np.float32)
    for cix in range(NCORES):
        out[perms[cix]] = np.asarray(res.results[cix]["y"], np.float32)
    if DBG.get("keep"):
        DBG["res"] = res
        DBG["perms"] = perms
    return out.reshape(B, S, D)
